# version 14 (final)
# baseline (speedup 1.0000x reference)
import numpy as np
import ml_dtypes
from contextlib import ExitStack
import concourse.bass as bass
import concourse.mybir as mybir
from concourse.bass_utils import run_bass_kernel_spmd

F32 = mybir.dt.float32
BF16 = mybir.dt.bfloat16
AF = mybir.ActivationFunctionType
ALU = mybir.AluOpType

NCORE = 8
DM = 1024
DI = 2048
NST = 16
DTR = 64
FF = 4096
NJ = 8
TS = 512
WIN = 4096
HW = 2048
DSL = DI // NCORE
EPS = 1e-5

ENGINES = ("sync", "scalar", "vector", "tensor", "gpsimd")


class Prog:
    def __init__(self, nc, es):
        self.nc = nc
        self.streams = {e: [] for e in ENGINES}
        self.semh = {}
        self.count = {}
        for s in ("dma", "pe", "act", "dve", "gp"):
            self.semh[s] = es.enter_context(nc.semaphore("sem_" + s))
            self.count[s] = 0

    def op(self, eng, fn, waits=(), inc=None, amt=1):
        flat = []
        for w in waits:
            if w is None:
                continue
            if isinstance(w, list):
                flat.extend(x for x in w if x is not None)
            else:
                flat.append(w)
        waits = tuple(flat)
        tok = None
        if inc is not None:
            self.count[inc] += amt
            tok = (inc, self.count[inc])
        semh = self.semh

        def run(e, _fn=fn, _waits=waits, _inc=inc, _amt=amt):
            for s, v in _waits:
                e.wait_ge(semh[s], v)
            inst = _fn(e)
            if _inc is not None and inst is not None:
                inst.then_inc(semh[_inc], _amt)
        self.streams[eng].append(run)
        return tok

    def tok(self, name):
        return (name, self.count[name])


DEBUG_OUTS = ()
DEBUG_TAP = False
SER_WS = False


def build_program():
    global SER_WS
    nc = bass.Bass("TRN2", target_bir_lowering=False)

    def din(name, shape, dt=F32):
        return nc.dram_tensor(name, list(shape), dt, kind="ExternalInput").ap()

    def dout(name, shape, dt=F32):
        return nc.dram_tensor(name, list(shape), dt, kind="ExternalOutput").ap()

    _dreg = {}

    def dint(name, shape, dt=F32, shared=False):
        kw = dict(addr_space="Shared") if shared else {}
        ap = nc.dram_tensor(name, list(shape), dt, kind="Internal", **kw).ap()
        _dreg[name] = ap
        return ap

    xs = din("xs", [NJ, TS, DM])
    w1 = din("w1", [DM, FF])
    w2 = din("w2", [FF, DM])
    winT = din("winT", [DM, 2 * DSL])
    wxpT = din("wxpT", [DSL, 96])
    wdtT = din("wdtT", [DTR, DSL])
    woutT = din("woutT", [DI, DM])
    b1r = din("b1r", [128, 32])
    b2r = din("b2r", [128, 8])
    convw_r = din("convw_r", [128, 8])
    alog_r = din("alog_r", [128, 32])
    cols_r = din("cols_r", [128, 8])
    ident_in = din("ident", [128, 128])
    ys = dout("ys", [NJ, TS, DM])

    w1b = dint("w1b", [DM, FF], BF16)
    w2b = dint("w2b", [FF, DM], BF16)
    winTb = dint("winTb", [DM, 2 * DSL], BF16)
    wxpTb = dint("wxpTb", [DSL, 96], BF16)
    wdtTb = dint("wdtTb", [DTR, DSL], BF16)
    woutTb = dint("woutTb", [DI, DM], BF16)
    x1_scr = dint("x1_scr", [NJ, 128, FF])
    if DEBUG_TAP:
        xin_scr = dint("xin_scr", [NJ, 128, FF])
        ln1_in = [dint(f"ln1_in{j}", [DM, TS], BF16) for j in range(NJ)]
    ln2_in = [dint(f"ln2_in{j}", [DM, TS], BF16) for j in range(NJ)]
    ln2_out = [dint(f"ln2_out{j}", [NCORE * DM, TS], BF16, shared=True) for j in range(NJ)]
    xp_in = [[dint(f"xp_in{j}_{h}", [96, HW]) for h in range(2)] for j in range(NJ)]
    xp_out = [[dint(f"xp_out{j}_{h}", [96, HW], shared=True) for h in range(2)] for j in range(NJ)]
    y_in = [dint(f"y_in{j}", [2 * 128, WIN], BF16) for j in range(NJ)]
    y_out = [dint(f"y_out{j}", [DI, WIN], BF16, shared=True) for j in range(NJ)]

    es = ExitStack()

    def sbt(name, shape, dt=F32):
        return es.enter_context(nc.sbuf_tensor(name, list(shape), dt))

    def pst(name, shape):
        return es.enter_context(nc.psum_tensor(name, list(shape), F32))

    aXin = sbt("aXin", [128, 2, 2052])
    aRes = sbt("aRes", [128, 2, 2048])
    aU = sbt("aU", [128, 2, 2048])
    aWt = sbt("aWt", [128, 2, 2048])
    aWin = sbt("aWin", [128, 2048])
    aDA = sbt("aDA", [128, 2048])
    aBB = sbt("aBB", [128, 2048])
    aHH = sbt("aHH", [128, 2048])
    aBrep = sbt("aBrep", [128, 2048])
    aCrep = sbt("aCrep", [128, 2048])
    aYa = sbt("aYa", [128, 2, 2048])
    aSp = sbt("aSp", [128, 1536])
    aXfm = sbt("aXfm", [128, 4096])
    aHb = sbt("aHb", [128, 8192])
    aLnb = sbt("aLnb", [128, 2048])
    aYbec = sbt("aYbec", [128, 512])
    aWs = sbt("aWs", [128, 2048])
    aSio = sbt("aSio", [128, 1024])
    ident = sbt("identt", [128, 128])
    onescol = sbt("onescol", [128, 1])
    onesrow = sbt("onesrow", [1, 128])
    onesmat = sbt("onesmat", [128, 128])
    carry = sbt("carry", [128, 2 * NST])
    convtail = sbt("convtail", [128, 8])
    smallp = sbt("smallp", [128, 64])
    apar = sbt("apar", [128, 64])
    wxp_sb = sbt("wxp_sb", [128, 2 * 96], BF16)
    wdt_sb = sbt("wdt_sb", [DTR, DSL], BF16)
    flsc = sbt("flsc", [128, 8])

    import os
    if os.environ.get("KPRINT"):
        print("SBUF after arenas: base", nc.sbuf_base, "top", nc.sbuf_top,
              "remaining", nc.sbuf_bytes_remaining)

    pL = pst("pL", [128, 1024])
    pA = pst("pA", [128, 1024])
    pB = pst("pB", [128, 1024])
    pT = pst("pT", [128, 1024])

    hb_t = aHb[:, 0:8192].bitcast(BF16).rearrange("p (oc f) -> p oc f", oc=32)
    lnb_t = aLnb[:, 0:2048].bitcast(BF16).rearrange("p (ck f) -> p ck f", ck=8)
    rows = aLnb[0:1, :]
    xfm = aXfm
    sq_chunk = aSio[:, 0:512]
    winTb_t = aWin[:, 0:2048].bitcast(BF16).rearrange("p (ck er) -> p ck er", ck=8)
    stage_t = aU[:, 0, 0:2048].bitcast(BF16).rearrange("p (ck f) -> p ck f", ck=8)
    ws_t = aWs[:, 0:2048].bitcast(BF16)
    ybec_t = aYbec[:, 0:512].bitcast(BF16)
    sp_m = aSp[:, 0:512]
    sp_t = aSp[:, 512:1024]
    sp_a = aSp[:, 1024:1536]
    ub_t = aSp[:, 0:512].bitcast(BF16)
    ygb_t = aSp[:, 0:1024].bitcast(BF16)
    dtin_f = aYa[0:DTR, 0, 0:2048]
    dtin_b = aYa[0:DTR, 1, 0:1024].bitcast(BF16)

    prog = Prog(nc, es)
    P = prog.op
    T = prog.tok
    _pid = {}

    def pid():
        if "v" not in _pid:
            _pid["v"] = nc.partition_id()
        return _pid["v"]

    def DMA(fn, waits=()):
        return P("sync", fn, waits, inc="dma", amt=16)

    def PE(fn, waits=()):
        return P("tensor", fn, waits, inc="pe")

    def ACT(fn, waits=()):
        return P("scalar", fn, waits, inc="act")

    def DVE(fn, waits=()):
        return P("vector", fn, waits, inc="dve")

    def COLL(fn, waits=()):
        return P("gpsimd", fn, waits, inc="gp", amt=1)

    def CGATE():
        return T("gp")

    def PEFENCE(slot=None):
        if slot is None:
            slot = st["fence_slot"] = 1 - st.get("fence_slot", 1)
        t = PE(lambda e, slot=slot: e.matmul(pT[0:1, slot * 512:slot * 512 + 16],
                                             onescol[:], ident[:, 0:16],
                                             start=True, stop=True),
               waits=[st["pT"][slot], t_id0])
        st["pT"][slot] = t
        return t

    def DVEFL():
        return DVE(lambda e: e.memset(flsc[:, 0:1], 0.0))

    def ACTFL():
        return ACT(lambda e: e.copy(out=flsc[:, 1:2], in_=flsc[:, 2:3]))

    def DVESC(ap, waits=()):
        return DVE(lambda e, ap=ap: e.tensor_copy(out=ap, in_=ap), waits=waits)

    def ACTSC(ap, waits=()):
        return ACT(lambda e, ap=ap: e.copy(out=ap, in_=ap), waits=waits)

    st = {
        "pA": [None, None], "pB": [None, None], "pT": [None, None],
        "pL0": None, "pL1": None,
        "ws": [None, None], "ybec": [None, None],
        "xfm_r": None, "lnb_r": None, "hb_r": None, "sio_r": None,
        "sq_r": None,
    }
    prev = {
        "xin_r": None, "res_r": None, "u_r": None, "wt_r": None, "stage_r": None,
        "dA_r": None, "bb_r": None, "hh_r": None, "brep_r": None, "crep_r": None,
        "ya_r": [None, None], "sp_r": None, "ygb_r": None, "dtin_r": None,
        "dtadd": None, "spm_r": None,
    }

    t_id0 = t_id = DMA(lambda e: e.dma_start(out=ident[:], in_=ident_in))
    DVE(lambda e: e.memset(onescol[:], 1.0))
    DVE(lambda e: e.memset(onesrow[:], 1.0))
    t_ones = DVE(lambda e: e.memset(onesmat[:], 1.0))
    DMA(lambda e: e.dma_start(out=apar[:, 0:32], in_=alog_r))
    DMA(lambda e: e.dma_start(out=apar[:, 32:40], in_=convw_r))
    DMA(lambda e: e.dma_start(out=smallp[:, 0:32], in_=b1r))
    DMA(lambda e: e.dma_start(out=smallp[:, 50:58], in_=b2r))
    DMA(lambda e: e.dma_start(out=smallp[:, 40:48], in_=cols_r))
    t0 = T("dma")
    ta0 = ACT(lambda e: e.activation(out=apar[:, 0:32], in_=apar[:, 0:32], func=AF.Exp), waits=[t0])
    ta0f = ACTSC(apar[:, 0:32])
    DVE(lambda e: e.tensor_scalar_mul(out=apar[:, 0:32], in0=apar[:, 0:32], scalar1=-1.0), waits=[ta0f])
    DVE(lambda e: e.tensor_scalar_mul(out=smallp[:, 32:40], in0=smallp[:, 50:58], scalar1=0.5), waits=[t0])
    DVE(lambda e: e.memset(smallp[:, 48:49], EPS))
    DVE(lambda e: e.memset(smallp[:, 49:50], 1.0))
    nc.const_aps.register(F32, EPS, smallp[:, 48:49])
    nc.const_aps.register(F32, 1.0, smallp[:, 49:50])
    t_small = T("dve")

    A_col = lambda dg, n: apar[:, dg * 16 + n:dg * 16 + n + 1]
    cw_col = lambda dg, k: apar[:, 32 + dg * 4 + k:33 + dg * 4 + k]
    b1_col = lambda oc: smallp[:, oc:oc + 1]
    b2h_col = lambda mc: smallp[:, 32 + mc:33 + mc]
    cb_col = lambda dg: smallp[:, 40 + dg:41 + dg]
    dtb_col = lambda dg: smallp[:, 42 + dg:43 + dg]
    dp_col = lambda dg: smallp[:, 44 + dg:45 + dg]

    cbf = aHb[:, 0:4096]
    cbb = aXfm[:, 0:2048].bitcast(BF16)
    last_cast_read = [None]

    def cast_weight(src, dst, nrows, ncols, scale=None):
        for rc in range(-(-nrows // 128)):
            r0 = rc * 128
            pr = min(128, nrows - r0)
            for cc in range(-(-ncols // 4096)):
                c0 = cc * 4096
                fc = min(4096, ncols - c0)
                tl = DMA(lambda e, r0=r0, pr=pr, c0=c0, fc=fc: e.dma_start(
                    out=cbf[0:pr, 0:fc], in_=src[r0:r0 + pr, c0:c0 + fc]),
                    waits=[last_cast_read[0]])
                if scale is None:
                    tc_ = DVE(lambda e, pr=pr, fc=fc: e.tensor_copy(
                        out=cbb[0:pr, 0:fc], in_=cbf[0:pr, 0:fc]), waits=[tl])
                else:
                    tc_ = DVE(lambda e, pr=pr, fc=fc, s=scale: e.tensor_scalar_mul(
                        out=cbb[0:pr, 0:fc], in0=cbf[0:pr, 0:fc], scalar1=s), waits=[tl])
                tcf = DVESC(cbb[0:pr, 0:fc])
                DMA(lambda e, r0=r0, pr=pr, c0=c0, fc=fc: e.dma_start(
                    out=dst[r0:r0 + pr, c0:c0 + fc], in_=cbb[0:pr, 0:fc]), waits=[tcf])
                last_cast_read[0] = tc_
        return T("dma")

    cast_weight(w1, w1b, DM, FF)
    cast_weight(w2, w2b, FF, DM, scale=0.5)
    cast_weight(winT, winTb, DM, 2 * DSL)
    cast_weight(wxpT, wxpTb, DSL, 96)
    cast_weight(wdtT, wdtTb, DTR, DSL)
    t_wcast = cast_weight(woutT, woutTb, DI, DM)
    st["hb_r"] = last_cast_read[0]
    st["xfm_r"] = t_wcast

    tlwin = DMA(lambda e: e.dma_start(out=winTb_t[:, :, :],
                                      in_=winTb.rearrange("(ck p) er -> p ck er", p=128)), waits=[t_wcast])
    tlwxp = DMA(lambda e: e.dma_start(out=wxp_sb.rearrange("p (g r) -> p g r", g=2),
                                      in_=wxpTb.rearrange("(g p) r -> p g r", p=128)), waits=[t_wcast])
    tlwdt = DMA(lambda e: e.dma_start(out=wdt_sb[:], in_=wdtTb), waits=[t_wcast])

    def layernorm(dep_toks):
        deps = list(dep_toks)
        sqb = [aSio[:, 0:512], aSio[:, 512:1024]]
        tq = ACT(lambda e: e.activation(out=sqb[0], in_=xfm[:, 0:512], func=AF.Square),
                 waits=deps + [st["pL1"], st["sq_r"], st["sio_r"]])
        tqf = ACTSC(sqb[0])
        for ck in range(8):
            if ck < 7:
                tq = ACT(lambda e, ck=ck: e.activation(
                    out=sqb[(ck + 1) % 2], in_=xfm[:, (ck + 1) * 512:(ck + 2) * 512],
                    func=AF.Square),
                    waits=[st["sq_r"], st["sio_r"] if ck == 0 else None])
            st["sq_r"] = PE(lambda e, ck=ck: e.matmul(pL[:, 512:1024], onesmat[:], sqb[ck % 2],
                                                      start=(ck == 0), stop=(ck == 7)),
                            waits=[tqf, CGATE() if ck == 0 else None])
            if ck < 7:
                tqf = ACTSC(sqb[(ck + 1) % 2])
        for ck in range(8):
            PE(lambda e, ck=ck: e.matmul(pL[:, 0:512], onesmat[:], xfm[:, ck * 512:(ck + 1) * 512],
                                         start=(ck == 0), stop=(ck == 7)),
               waits=(deps + [st["pL0"], t_ones, CGATE()]) if ck == 0 else [])
        t_stats = PEFENCE()
        lm, lv, lr = aHb[:, 0:512], aHb[:, 512:1024], aHb[:, 1024:1536]
        st["pL0"] = DVE(lambda e: e.tensor_scalar_mul(out=lm, in0=pL[:, 0:512], scalar1=1.0 / DM),
                        waits=[t_stats])
        st["pL1"] = DVE(lambda e: e.tensor_scalar_mul(out=lv, in0=pL[:, 512:1024], scalar1=1.0 / DM))
        DVE(lambda e: e.tensor_mul(out=lr, in0=lm, in1=lm))
        DVE(lambda e: e.tensor_sub(out=lv, in0=lv, in1=lr), waits=[T("dve")])
        t4 = T("dve")
        t4f = DVESC(lv)
        t5 = ACT(lambda e: e.activation(out=lr, in_=lv, func=AF.Ln, bias=EPS), waits=[t4f])
        t6 = ACT(lambda e: e.activation(out=lr, in_=lr, func=AF.Exp, scale=-0.5), waits=[t5])
        t6f = ACTSC(lr)
        t7 = DVE(lambda e: e.tensor_mul(out=lm, in0=lm, in1=lr), waits=[t6f])
        tn = None
        for ck in range(8):
            ta_ = DVE(lambda e, ck=ck: e.tensor_mul(out=aSio[:, 512:1024],
                                                    in0=xfm[:, ck * 512:(ck + 1) * 512],
                                                    in1=lr),
                      waits=[t7, st["sq_r"]] if ck == 0 else [])
            tn = DVE(lambda e, ck=ck: e.tensor_sub(out=lnb_t[:, ck, :],
                                                   in0=aSio[:, 512:1024],
                                                   in1=lm),
                     waits=[ta_, st["lnb_r"] if ck == 0 else None])
        tsc = DVESC(aLnb[:, 1792:2048])
        st["lnb_r"] = None
        st["sio_r"] = tn
        return [tsc]

    w1_src = w1b.rearrange("(ck p) f -> p ck f", p=128)
    w2_src = w2b.rearrange("(oc p) f -> p oc f", p=128)
    ws_ck = lambda slot: ws_t[:, slot * 1024:(slot + 1) * 1024].rearrange("p (ck f) -> p ck f", ck=8)

    def ffn_block(t_ln):
        pend = None
        for oc in range(32):
            slot = oc % 2
            bank = oc % 2
            tws = DMA(lambda e, oc=oc, slot=slot: e.dma_start(
                out=ws_ck(slot)[:, :, :],
                in_=w1_src[:, :, oc * 128:(oc + 1) * 128]),
                waits=[st["ws"][slot], t_wcast])
            for ck in range(8):
                PE(lambda e, oc=oc, ck=ck, bank=bank, slot=slot: e.matmul(
                    pB[:, bank * 512:(bank + 1) * 512],
                    ws_ck(slot)[:, ck, :], lnb_t[:, ck, :],
                    start=(ck == 0), stop=(ck == 7)),
                   waits=[tws, t_ln, st["pB"][bank], CGATE()] if ck == 0 else [])
            st["ws"][slot] = tmm = T("pe")
            if pend is not None:
                poc, pbank, ptok = pend
                st["pB"][pbank] = ACT(lambda e, oc=poc, bank=pbank: e.activation(
                    out=hb_t[:, oc, :], in_=pB[:, bank * 512:(bank + 1) * 512],
                    func=AF.Relu, bias=b1_col(oc)),
                    waits=[ptok, tmm, t_small, st["hb_r"]] if poc < 2 else [ptok, tmm])
            pend = (oc, bank, tmm)
        tf = PEFENCE()
        poc, pbank, ptok = pend
        st["pB"][pbank] = ACT(lambda e, oc=poc, bank=pbank: e.activation(
            out=hb_t[:, oc, :], in_=pB[:, bank * 512:(bank + 1) * 512],
            func=AF.Relu, bias=b1_col(oc)), waits=[ptok, tf])
        t_h = T("act")
        st["hb_r"] = None
        pend = None
        for mc in range(8):
            bank = mc % 2
            for ocg in range(4):
                slot = (mc * 4 + ocg) % 2
                tws = DMA(lambda e, mc=mc, ocg=ocg, slot=slot: e.dma_start(
                    out=ws_ck(slot)[:, :, :],
                    in_=w2_src[:, ocg * 8:(ocg + 1) * 8, mc * 128:(mc + 1) * 128]),
                    waits=[st["ws"][slot]])
                for k in range(8):
                    oc = ocg * 8 + k
                    PE(lambda e, oc=oc, k=k, bank=bank, slot=slot: e.matmul(
                        pB[:, bank * 512:(bank + 1) * 512],
                        ws_ck(slot)[:, k, :], hb_t[:, oc, :],
                        start=(oc == 0), stop=(oc == 31)),
                       waits=([tws, t_h, st["pB"][bank], CGATE()] if oc == 0 else ([tws] if k == 0 else [])))
                st["ws"][slot] = T("pe")
            tmm = T("pe")
            if pend is not None:
                pmc, pbank, ptok = pend
                st["pB"][pbank] = DVE(lambda e, mc=pmc, bank=pbank: e.scalar_tensor_tensor(
                    out=xfm[:, mc * 512:(mc + 1) * 512],
                    in0=pB[:, bank * 512:(bank + 1) * 512], scalar=b2h_col(mc),
                    in1=xfm[:, mc * 512:(mc + 1) * 512], op0=ALU.add, op1=ALU.add),
                    waits=[ptok, tmm, t_small])
            pend = (mc, bank, tmm)
        tf = PEFENCE()
        pmc, pbank, ptok = pend
        st["pB"][pbank] = DVE(lambda e, mc=pmc, bank=pbank: e.scalar_tensor_tensor(
            out=xfm[:, mc * 512:(mc + 1) * 512],
            in0=pB[:, bank * 512:(bank + 1) * 512], scalar=b2h_col(mc),
            in1=xfm[:, mc * 512:(mc + 1) * 512], op0=ALU.add, op1=ALU.add),
            waits=[ptok, tf, t_small])
        T("dve")
        t_f = DVESC(xfm[:, 3584:4096])
        return [t_f]

    t_ag = []
    t_prev_lnread = None
    t_prev_stash = None
    for j in range(NJ):
        t_tr_last = None
        for tt in range(4):
            tl = DMA(lambda e, j=j, tt=tt: e.dma_start(out=aSio[:, 0:1024],
                                                       in_=xs[j, tt * 128:(tt + 1) * 128, :]),
                     waits=[t_tr_last, st["sio_r"]])
            pend_tr = None
            for ck in range(8):
                slot = ck % 2
                tp = PE(lambda e, ck=ck, slot=slot: e.transpose(
                    pT[:, slot * 512:slot * 512 + 128], aSio[:, ck * 128:(ck + 1) * 128], ident[:]),
                    waits=[tl, t_id, st["pT"][slot], t_prev_lnread, t_prev_stash, CGATE(),
                           st["xfm_r"] if ck < 2 else None])
                if pend_tr is not None:
                    pck, pslot, ptok = pend_tr
                    st["pT"][pslot] = ACT(lambda e, ck=pck, tt=tt, slot=pslot: e.copy(
                        out=xfm[:, ck * 512 + tt * 128: ck * 512 + (tt + 1) * 128],
                        in_=pT[:, slot * 512:slot * 512 + 128]), waits=[ptok, tp])
                pend_tr = (ck, slot, tp)
            pck, pslot, ptok = pend_tr
            tf = PEFENCE(slot=1 - pslot)
            st["pT"][pslot] = ACT(lambda e, ck=pck, tt=tt, slot=pslot: e.copy(
                out=xfm[:, ck * 512 + tt * 128: ck * 512 + (tt + 1) * 128],
                in_=pT[:, slot * 512:slot * 512 + 128]), waits=[ptok, tf])
            ACTSC(xfm[:, 7 * 512 + tt * 128: 7 * 512 + (tt + 1) * 128])
            t_tr_last = T("pe")
            st["sio_r"] = t_tr_last
        t_x = T("act")
        st["xfm_r"] = None
        t_ln1 = layernorm([t_x, ACTFL()])
        if DEBUG_TAP:
            t_xs = DMA(lambda e, j=j: e.dma_start(out=xin_scr[j], in_=xfm[:, 0:4096]),
                       waits=[t_ln1])
            tl1 = DMA(lambda e, j=j: e.dma_start(
                out=ln1_in[j].rearrange("(ck p) t -> p ck t", p=128),
                in_=lnb_t[:, :, :]), waits=[t_ln1])
            st["lnb_r"] = tl1
            t_ln1 = t_ln1 + [t_xs]
        t_x1 = ffn_block(t_ln1)
        t_prev_stash = DMA(lambda e, j=j: e.dma_start(out=x1_scr[j], in_=xfm[:, 0:4096]), waits=[t_x1])
        t_ln2 = layernorm([t_x1])
        t_prev_lnread = t_ln2
        tout = DMA(lambda e, j=j: e.dma_start(
            out=ln2_in[j].rearrange("(ck p) t -> p ck t", p=128),
            in_=lnb_t[:, :, :]), waits=[t_ln2])
        st["lnb_r"] = tout
        t_ag.append(COLL(lambda e, j=j: e.collective_compute(
            "AllGather", ALU.bypass, replica_groups=[list(range(NCORE))],
            ins=[ln2_in[j].opt()], outs=[ln2_out[j].opt()]), waits=[tout]))
    st["xfm_r"] = t_prev_lnread

    xin3 = aXin
    wout_src = woutTb.rearrange("(ec p) o -> p ec o", p=128)
    ws_ec = lambda slot: ws_t[:, slot * 1024:(slot + 1) * 1024].rearrange("p (ec f) -> p ec f", ec=8)
    t_prev_out = t_prev_stash
    for j in range(NJ):
        t_yg_dma = None
        tg1 = tsi = None
        for h in range(2):
            toff = h * HW
            batch_start = (j % 4 == 0) and h == 0
            if batch_start:
                t_tail = DVE(lambda e: e.memset(xin3[:, :, 0:3], 0.0), waits=[prev["xin_r"]])
                t_cm = DVE(lambda e: e.memset(carry[:], 0.0), waits=[prev["hh_r"]])
            else:
                t_tail = DVE(lambda e: e.tensor_copy(
                    out=xin3[:, :, 0:3],
                    in_=convtail.rearrange("p (g k) -> p g k", g=2)[:, :, 0:3]),
                    waits=[prev["xin_r"]])
                t_cm = None
            pend = None

            def _ip_evac(p_, extra):
                per, psct, pbank, ptok = p_
                if per < 2:
                    return ACT(lambda e, er=per, sct=psct, bank=pbank: e.copy(
                        out=xin3[:, er, 3 + sct * 512: 3 + (sct + 1) * 512],
                        in_=pA[:, bank * 512:(bank + 1) * 512]),
                        waits=[ptok, t_tail] + extra)
                return ACT(lambda e, er=per, sct=psct, bank=pbank: e.copy(
                    out=aRes[:, er - 2, sct * 512:(sct + 1) * 512],
                    in_=pA[:, bank * 512:(bank + 1) * 512]),
                    waits=[ptok, prev["res_r"]] + extra)

            for sct in range(4):
                src = h * 4 + sct
                tsg = DMA(lambda e, j=j, src=src: e.dma_start(
                    out=stage_t[:, :, :],
                    in_=ln2_out[j][src * DM:(src + 1) * DM, :].rearrange("(ck p) t -> p ck t", p=128)),
                    waits=[t_ag[j], tlwin, prev["stage_r"], prev["u_r"]])
                for er in range(4):
                    bank = er % 2
                    for ck in range(8):
                        PE(lambda e, er=er, ck=ck, bank=bank: e.matmul(
                            pA[:, bank * 512:(bank + 1) * 512],
                            winTb_t[:, ck, er * 128:(er + 1) * 128], stage_t[:, ck, :],
                            start=(ck == 0), stop=(ck == 7)),
                           waits=[tsg, st["pA"][bank], CGATE()] if ck == 0 else [])
                    tmm = T("pe")
                    if pend is not None:
                        st["pA"][pend[2]] = _ip_evac(pend, [tmm])
                    pend = (er, sct, bank, tmm)
                prev["stage_r"] = T("pe")
            tf = PEFENCE(slot=1 - pend[2])
            st["pA"][pend[2]] = _ip_evac(pend, [tf])
            t_ip = T("act")
            t_ct = DVE(lambda e: e.tensor_copy(
                out=convtail.rearrange("p (g k) -> p g k", g=2)[:, :, 0:3],
                in_=xin3[:, :, 2048:2051]), waits=[t_ip])
            t1e = [None, None]
            for dg in range(2):
                cacc = aDA if dg == 0 else aBB
                t1 = DVE(lambda e, dg=dg, cacc=cacc: e.tensor_scalar(
                    out=cacc[:, :], in0=xin3[:, dg, 0:2048], scalar1=cw_col(dg, 0), scalar2=cb_col(dg),
                    op0=ALU.mult, op1=ALU.add),
                    waits=[t_ip, t_small, prev["dA_r" if dg == 0 else "bb_r"]])
                for k in range(1, 4):
                    t1 = DVE(lambda e, dg=dg, k=k, cacc=cacc: e.scalar_tensor_tensor(
                        out=cacc[:, :], in0=xin3[:, dg, k:k + 2048], scalar=cw_col(dg, k),
                        in1=cacc[:, :], op0=ALU.mult, op1=ALU.add), waits=[t1])
                t1e[dg] = t1
            t1f = DVEFL()
            t_u = []
            for dg in range(2):
                cacc = aDA if dg == 0 else aBB
                t_u.append(ACT(lambda e, dg=dg, cacc=cacc: e.activation(
                    out=aU[:, dg, :], in_=cacc[:, :], func=AF.Silu),
                    waits=[t1e[dg], t1f, prev["u_r"] if dg == 0 else None,
                           prev["stage_r"] if dg == 0 else None]))
            t_uf = ACTSC(aU[:, 1, 0:512])
            prev["dA_r"] = t_u[0]
            prev["bb_r"] = t_u[1]
            t_xp_last = None
            t_xpe = None
            for sct in range(4):
                tc1 = DVE(lambda e, sct=sct: e.tensor_copy(out=ub_t[:, 0:512],
                                                           in_=aU[:, 0, sct * 512:(sct + 1) * 512]),
                          waits=[t_u[0], t_xpe,
                                 [prev["sp_r"], t_uf] if sct == 0 else None])
                tc2 = DVE(lambda e, sct=sct: e.tensor_copy(out=ub_t[:, 512:1024],
                                                           in_=aU[:, 1, sct * 512:(sct + 1) * 512]),
                          waits=[t_u[1], t_uf if sct == 0 else None])
                for dg in range(2):
                    PE(lambda e, dg=dg: e.matmul(pA[0:96, 0:512], wxp_sb[:, dg * 96:(dg + 1) * 96],
                                                 ub_t[:, dg * 512:(dg + 1) * 512],
                                                 start=(dg == 0), stop=(dg == 1)),
                       waits=[tc1, tc2, tlwxp, st["pA"][0], CGATE()] if dg == 0 else [])
                tmm = PEFENCE(slot=1)
                st["pA"][0] = t_xpe = ACT(lambda e: e.copy(out=aSio[0:96, 0:512], in_=pA[0:96, 0:512]),
                                          waits=[tmm, st["sio_r"], st["sq_r"]])
                t_xpf = ACTSC(aSio[0:96, 0:512])
                st["sio_r"] = t_xp_last = DMA(lambda e, j=j, h=h, sct=sct: e.dma_start(
                    out=xp_in[j][h][:, sct * 512:(sct + 1) * 512], in_=aSio[0:96, 0:512]),
                    waits=[t_xpf])
            t_ar = COLL(lambda e, j=j, h=h: e.collective_compute(
                "AllReduce", ALU.add, replica_groups=[list(range(NCORE))],
                ins=[xp_in[j][h].opt()], outs=[xp_out[j][h].opt()]), waits=[t_xp_last])
            tdl = DMA(lambda e, j=j, h=h: e.dma_start(
                out=dtin_f[:, :], in_=xp_out[j][h][0:DTR, :]),
                waits=[t_ar, prev["dtin_r"], prev["ya_r"][0], prev["ya_r"][1]])
            tdc = DVE(lambda e: e.tensor_copy(out=dtin_b[:, 0:HW], in_=dtin_f[:, :]), waits=[tdl])
            tdcf = DVESC(aYa[0:DTR, 1, 0:1024])
            prev["dtin_r"] = tdc
            pend_dt = [None]

            def _dt_chain(pdg, ptc, pbank, extra):
                txb = DVE(lambda e, bank=pbank, dg=pdg: e.tensor_scalar_add(
                    out=sp_t[:, :], in0=pA[:, bank * 512:(bank + 1) * 512],
                    scalar1=dtb_col(dg)), waits=extra + [prev["dtadd"], prev["sp_r"]])
                st["pA"][pbank] = txb
                tmx = DVE(lambda e: e.tensor_scalar_max(out=sp_m[:, :], in0=sp_t[:, :], scalar1=0.0),
                          waits=[txb, prev["spm_r"]])
                ta1 = ACT(lambda e: e.activation(out=sp_a[:, :], in_=sp_t[:, :], func=AF.Abs),
                          waits=[tmx, prev["dtadd"]])
                ta2 = ACT(lambda e: e.activation(out=sp_t[:, :], in_=sp_a[:, :], func=AF.Exp,
                                                 scale=-1.0), waits=[ta1, tmx])
                ta3 = ACT(lambda e: e.activation(out=sp_a[:, :], in_=sp_t[:, :], func=AF.Ln, bias=1.0),
                          waits=[ta2])
                ta3f = ACTSC(sp_a[:, :])
                prev["sp_r"] = ta3f
                prev["dtadd"] = prev["spm_r"] = DVE(
                    lambda e, dg=pdg, tc=ptc: e.tensor_add(
                        out=xin3[:, dg, tc * 512:(tc + 1) * 512],
                        in0=sp_m[:, :], in1=sp_a[:, :]), waits=[ta3f, tmx])

            for dg in range(2):
                for tc in range(4):
                    bank = tc % 2
                    tm = PE(lambda e, dg=dg, tc=tc, bank=bank: e.matmul(
                        pA[:, bank * 512:(bank + 1) * 512],
                        wdt_sb[:, dg * 128:(dg + 1) * 128], dtin_b[0:DTR, tc * 512:(tc + 1) * 512],
                        start=True, stop=True), waits=[tdcf, tlwdt, st["pA"][bank], CGATE()])
                    if pend_dt[0] is not None:
                        pdg, ptc, pbank, ptok = pend_dt[0]
                        _dt_chain(pdg, ptc, pbank, [ptok, tm])
                    pend_dt[0] = (dg, tc, bank, tm)
            tfd = PEFENCE(slot=1 - pend_dt[0][2])
            pdg, ptc, pbank, ptok = pend_dt[0]
            _dt_chain(pdg, ptc, pbank, [ptok, tfd])
            t_dt = T("dve")
            for dg in range(2):
                DVE(lambda e, dg=dg: e.tensor_mul(
                    out=aWt[:, dg, :], in0=xin3[:, dg, 0:2048], in1=aU[:, dg, :]),
                    waits=[t_dt, prev["wt_r"]] if dg == 0 else [])
            t_wt = T("dve")
            for n in range(NST):
                tb = DMA(lambda e, j=j, h=h, n=n: e.dma_start(
                    out=aBrep[:, :], in_=xp_out[j][h][64 + n:65 + n, :].partition_broadcast(128)),
                    waits=[t_ar, prev["brep_r"]])
                tcp = DMA(lambda e, j=j, h=h, n=n: e.dma_start(
                    out=aCrep[:, :], in_=xp_out[j][h][80 + n:81 + n, :].partition_broadcast(128)),
                    waits=[prev["crep_r"]])
                for dg in range(2):
                    tda = ACT(lambda e, dg=dg, n=n: e.activation(
                        out=aDA[:, :], in_=xin3[:, dg, 0:2048], func=AF.Exp, scale=A_col(dg, n)),
                        waits=[t_wt, prev["dA_r"], t_small])
                    tdaf = ACTFL()
                    tbb = DVE(lambda e, dg=dg: e.tensor_mul(
                        out=aBB[:, :], in0=aWt[:, dg, :], in1=aBrep[:, :]),
                        waits=[t_wt, tb, prev["bb_r"]])
                    tts = DVE(lambda e, dg=dg, n=n: e.tensor_tensor_scan(
                        out=aHH[:, :], data0=aDA[:, :], data1=aBB[:, :],
                        initial=carry[:, dg * NST + n:dg * NST + n + 1],
                        op0=ALU.mult, op1=ALU.add), waits=[tdaf, prev["hh_r"], t_cm])
                    prev["dA_r"] = tts
                    prev["bb_r"] = tts
                    tcs = DVE(lambda e, dg=dg, n=n: e.tensor_copy(
                        out=carry[:, dg * NST + n:dg * NST + n + 1], in_=aHH[:, HW - 1:HW]), waits=[tts])
                    tgg = DVE(lambda e: e.tensor_mul(out=aHH[:, :], in0=aHH[:, :], in1=aCrep[:, :]),
                              waits=[tcp, tcs])
                    if n == 0:
                        tya = DVE(lambda e, dg=dg: e.tensor_copy(out=aYa[:, dg, :], in_=aHH[:, :]),
                                  waits=[tgg, prev["ya_r"][dg]])
                    else:
                        tya = DVE(lambda e, dg=dg: e.tensor_add(out=aYa[:, dg, :], in0=aYa[:, dg, :],
                                                                in1=aHH[:, :]), waits=[tgg])
                    prev["hh_r"] = tya
                    prev["brep_r"] = tbb
                    prev["crep_r"] = tgg
            for dg in range(2):
                tsi = ACT(lambda e, dg=dg: e.activation(
                    out=aBB[:, :], in_=aRes[:, dg, :], func=AF.Silu),
                    waits=[prev["bb_r"], prev["ygb_r"]])
                tsif = ACTSC(aBB[:, 0:1024])
                tg1 = DVE(lambda e, dg=dg: e.scalar_tensor_tensor(
                    out=aDA[:, :], in0=aU[:, dg, :], scalar=dp_col(dg),
                    in1=aYa[:, dg, :], op0=ALU.mult, op1=ALU.add),
                    waits=[prev["hh_r"], prev["dA_r"], t_small])
                prev["ya_r"][dg] = tg1
                tg2 = DVE(lambda e: e.tensor_mul(out=ygb_t[:, 0:2048], in0=aDA[:, :], in1=aBB[:, :]),
                          waits=[tg1, tsif, prev["ygb_r"], prev["spm_r"], prev["sp_r"]])
                prev["dA_r"] = tg2
                prev["bb_r"] = tg2
                prev["sp_r"] = tg2
                prev["spm_r"] = tg2
                t_fl = DVESC(aSp[:, 0:1024])
                t_yg_dma = DMA(lambda e, j=j, dg=dg, toff=toff: e.dma_start(
                    out=y_in[j][dg * 128:(dg + 1) * 128, toff:toff + HW],
                    in_=ygb_t[:, 0:2048]), waits=[t_fl])
                prev["ygb_r"] = t_yg_dma
                prev["sp_r"] = t_yg_dma
                prev["spm_r"] = t_yg_dma
            prev["u_r"] = tg1
            prev["res_r"] = tsi
            prev["wt_r"] = T("dve")
            prev["xin_r"] = T("act")
        t_yag = COLL(lambda e, j=j: e.collective_compute(
            "AllGather", ALU.bypass, replica_groups=[list(range(NCORE))],
            ins=[y_in[j].opt()], outs=[y_out[j].opt()]), waits=[t_yg_dma])

        tx1 = DMA(lambda e, j=j: e.dma_start(out=xfm[:, 0:4096], in_=x1_scr[j]),
                  waits=[t_prev_out, st["xfm_r"]])
        st["xfm_r"] = None
        for pair in range(4):
            twsl = [None, None]
            for q in range(2):
                mc = pair * 2 + q
                twsl[q] = DMA(lambda e, mc=mc, q=q: e.dma_start(
                    out=ws_ec2(ws_t, q)[:, :, :],
                    in_=wout_src[:, :, mc * 128:(mc + 1) * 128]),
                    waits=[st["ws"][0], st["ws"][1]])
            for ec in range(16):
                ysl = ec % 2
                tyb = DMA(lambda e, j=j, ec=ec, ysl=ysl: e.dma_start(
                    out=ybec_t[:, ysl * 512:(ysl + 1) * 512],
                    in_=y_out[j][ec * 128:(ec + 1) * 128, :][:, bass.ts(pid(), TS)]),
                    waits=[t_yag, st["ybec"][ysl]])
                for q in range(2):
                    PE(lambda e, ec=ec, q=q, ysl=ysl: e.matmul(
                        pB[:, q * 512:(q + 1) * 512],
                        ws_ec2(ws_t, q)[:, ec, :],
                        ybec_t[:, ysl * 512:(ysl + 1) * 512],
                        start=(ec == 0), stop=(ec == 15)),
                       waits=([tyb] + ([twsl[q], st["pB"][q], CGATE()] if ec == 0 else [])))
                st["ybec"][ysl] = T("pe")
            tg = PEFENCE()
            st["ws"][0] = tg
            st["ws"][1] = tg
            for q in range(2):
                mc = pair * 2 + q
                st["pB"][q] = DVE(lambda e, mc=mc, q=q: e.tensor_add(
                    out=xfm[:, mc * 512:(mc + 1) * 512],
                    in0=pB[:, q * 512:(q + 1) * 512],
                    in1=xfm[:, mc * 512:(mc + 1) * 512]), waits=[tg, tx1])
        T("dve")
        t_x2 = DVESC(xfm[:, 3584:4096])
        t_ln3 = layernorm([t_x2])
        t_out = ffn_block(t_ln3)
        for tt in range(4):
            pend_tr = None
            tev = None
            for ck in range(8):
                slot = ck % 2
                tp = PE(lambda e, ck=ck, tt=tt, slot=slot: e.transpose(
                    pT[:, slot * 512:slot * 512 + 128],
                    xfm[:, ck * 512 + tt * 128: ck * 512 + (tt + 1) * 128], ident[:]),
                    waits=[t_out, st["pT"][slot], t_prev_out, CGATE(),
                           st["sio_r"] if ck < 2 else None])
                if pend_tr is not None:
                    pck, pslot, ptok = pend_tr
                    st["pT"][pslot] = tev = ACT(lambda e, ck=pck, slot=pslot: e.copy(
                        out=aSio[:, ck * 128:(ck + 1) * 128], in_=pT[:, slot * 512:slot * 512 + 128]),
                        waits=[ptok, tp])
                pend_tr = (ck, slot, tp)
            pck, pslot, ptok = pend_tr
            tf = PEFENCE(slot=1 - pslot)
            st["pT"][pslot] = tev = ACT(lambda e, ck=pck, slot=pslot: e.copy(
                out=aSio[:, ck * 128:(ck + 1) * 128], in_=pT[:, slot * 512:slot * 512 + 128]),
                waits=[ptok, tf])
            t_evf = ACTSC(aSio[:, 0:1024])
            t_prev_out = DMA(lambda e, j=j, tt=tt: e.dma_start(
                out=ys[j, tt * 128:(tt + 1) * 128, :], in_=aSio[:, 0:1024]), waits=[t_evf])
        st["sio_r"] = t_prev_out
        st["xfm_r"] = T("pe")

    for name_ in DEBUG_OUTS:
        ap = _dreg[name_]
        od = nc.dram_tensor("dbg_" + name_, list(ap.shape), ap.dtype, kind="ExternalOutput").ap()
        DMA(lambda e, od=od, ap=ap: e.dma_start(out=od, in_=ap),
            waits=[prog.tok(s_) for s_ in ("dma", "pe", "act", "dve", "gp")])

    for e2 in ENGINES:
        waits = [prog.tok(s) for s in ("dma", "pe", "act", "dve", "gp")]
        prog.op(e2, lambda e: None, waits)

    with es:
        with nc.Block() as block:
            for name in ENGINES:
                def make(name):
                    def emit(eng):
                        for fn in prog.streams[name]:
                            fn(eng)
                    return emit
                getattr(block, name)(make(name))
    return nc


def ws_ec2(ws_t, slot):
    return ws_t[:, slot * 1024 * 2:(slot + 1) * 1024 * 2].rearrange("p (ec f) -> p ec f", ec=16)


_CACHE = {}


def kernel(L0_attn_x, L0_attn_in_proj_w, L0_attn_conv_w, L0_attn_conv_b, L0_attn_x_proj_w,
           L0_attn_dt_proj_w, L0_attn_dt_proj_b, L0_attn_A_log, L0_attn_D_param,
           L0_attn_out_proj_w, L0_ffn_w1, L0_ffn_b1, L0_ffn_w2, L0_ffn_b2):
    x = np.asarray(L0_attn_x)
    B, L, _ = x.shape
    x4d = np.ascontiguousarray(x.reshape(B * L // TS, TS, DM))
    c8 = np.ascontiguousarray

    if "nc" not in _CACHE:
        _CACHE["nc"] = build_program()
    nc = _CACHE["nc"]

    in_maps = []
    ident = np.eye(128, dtype=np.float32)
    w1_h = c8(L0_ffn_w1)
    w2_h = c8(L0_ffn_w2)
    b1r = c8(np.asarray(L0_ffn_b1).reshape(32, 128).T)
    b2r = c8(np.asarray(L0_ffn_b2).reshape(8, 128).T)
    woutT_h = c8(np.asarray(L0_attn_out_proj_w).T)
    for c in range(NCORE):
        sl = slice(c * DSL, (c + 1) * DSL)
        winT_h = c8(np.concatenate([L0_attn_in_proj_w[sl, :],
                                    L0_attn_in_proj_w[DI + c * DSL: DI + (c + 1) * DSL, :]],
                                   axis=0).T)
        wxpT_h = c8(np.asarray(L0_attn_x_proj_w)[:, sl].T)
        wdtT_h = c8(np.asarray(L0_attn_dt_proj_w)[sl, :].T)
        convw_r = c8(np.asarray(L0_attn_conv_w)[sl].reshape(2, 128, 4).transpose(1, 0, 2).reshape(128, 8))
        alog_r = c8(np.asarray(L0_attn_A_log)[sl].reshape(2, 128, 16).transpose(1, 0, 2).reshape(128, 32))
        cols = np.zeros((128, 8), np.float32)
        cols[:, 0:2] = np.asarray(L0_attn_conv_b)[sl].reshape(2, 128).T
        cols[:, 2:4] = np.asarray(L0_attn_dt_proj_b)[sl].reshape(2, 128).T
        cols[:, 4:6] = np.asarray(L0_attn_D_param)[sl].reshape(2, 128).T
        xs_c = c8(x4d[[jj * 8 + c for jj in range(NJ)]])
        in_maps.append(dict(
            xs=xs_c, w1=w1_h, w2=w2_h, winT=winT_h, wxpT=wxpT_h, wdtT=wdtT_h,
            woutT=woutT_h, b1r=b1r, b2r=b2r, convw_r=convw_r, alog_r=alog_r,
            cols_r=cols, ident=ident))

    res = run_bass_kernel_spmd(nc, in_maps, core_ids=list(range(NCORE)))
    out = np.empty((B * L // TS, TS, DM), np.float32)
    for c in range(NCORE):
        ysc = res.results[c]["ys"]
        for jj in range(NJ):
            out[jj * 8 + c] = ysc[jj]
    return out.reshape(B, L, DM)
